# version 08
# baseline (speedup 1.0000x reference)
import numpy as np
import ml_dtypes
import concourse.bass as bass
import concourse.mybir as mybir
from concourse.bass_utils import run_bass_kernel_spmd
from contextlib import ExitStack

F32 = mybir.dt.float32
BF16 = mybir.dt.bfloat16
AF = mybir.ActivationFunctionType
ALU = mybir.AluOpType

B, S, D = 4, 2048, 2048
NH, DH, LAT = 16, 128, 512
FFN, NE = 1024, 8
TOWN = 1024
NT = TOWN // 128
NKT = S // 128
EPS = 1e-5
ISQ = 1.0 / float(np.sqrt(128.0))
NEG = -1.0e30
NK512 = [1, 1, 2, 2, 3, 3, 4, 4]

TAP = None

ENGS = ("tensor", "vector", "scalar", "sync", "gpsimd")


class Sem:
    def __init__(self, h):
        self.h = h
        self.n = 0


class Rec:
    def __init__(self):
        self.ops = {e: [] for e in ENGS}

    def em(self, eng, fn, waits=(), inc=None, inc_k=1):
        waits = tuple((s, v) for (s, v) in waits if v > 0)
        if inc is not None:
            inc.n += inc_k

        def run(e, fn=fn, waits=waits, inc=inc, inc_k=inc_k):
            for (s, v) in waits:
                e.wait_ge(s.h, v)
            inst = fn(e)
            if inc is not None:
                inst.then_inc(inc.h, inc_k)
        self.ops[eng].append(run)


def build_nc():
    nc = bass.Bass("TRN2", target_bir_lowering=False)
    r = Rec()
    em = r.em

    def din(name, shape, dtype=F32):
        return nc.dram_tensor(name, list(shape), dtype, kind="ExternalInput").ap()

    xT = din("xT", [D, S])
    xTo = din("xTo", [D, TOWN])
    x_own = din("x_own", [TOWN, D])
    masks = din("masks", [NT, 128, 512], BF16)
    ident_b = din("ident_b", [128, 128], BF16)
    ident_f = din("ident_f", [128, 128], F32)
    wdq = din("wdq", [D, LAT]); wuq = din("wuq", [LAT, D])
    wdkv = din("wdkv", [D, LAT]); wukv = din("wukv", [LAT, 2 * D])
    woT = din("woT", [D, D])
    qg = din("qg", [LAT]); qb = din("qb", [LAT])
    kvg = din("kvg", [LAT]); kvb = din("kvb", [LAT])
    w1T = din("w1T", [NE, D, FFN]); w3T = din("w3T", [NE, D, FFN])
    w2T = din("w2T", [NE, FFN, D])
    wgT = din("wgT", [D, NE])
    bias_rep = din("bias_rep", [128, NE])
    sel8 = din("sel8", [8, 8, 128], BF16)
    x1pre_sp = nc.dram_tensor("x1pre_sp", [TOWN, D], F32, kind="Internal").ap()
    x1_sp = nc.dram_tensor("x1_sp", [TOWN, D], F32, kind="Internal").ap()
    y = nc.dram_tensor("y", [TOWN, D], F32, kind="ExternalOutput").ap()

    es0 = ExitStack()
    sg = Sem(es0.enter_context(nc.semaphore("sem_g")))
    ss = Sem(es0.enter_context(nc.semaphore("sem_s")))
    spe = Sem(es0.enter_context(nc.semaphore("sem_pe")))
    sdve = Sem(es0.enter_context(nc.semaphore("sem_dve")))
    sact = Sem(es0.enter_context(nc.semaphore("sem_act")))
    sspl = Sem(es0.enter_context(nc.semaphore("sem_spl")))
    sout = Sem(es0.enter_context(nc.semaphore("sem_out")))

    idb = es0.enter_context(nc.sbuf_tensor("idb", [128, 128], BF16))
    idf = es0.enter_context(nc.sbuf_tensor("idf", [128, 128], F32))
    gq_c = es0.enter_context(nc.sbuf_tensor("gq_c", [128, 4], F32))
    bq_c = es0.enter_context(nc.sbuf_tensor("bq_c", [128, 4], F32))
    gkv_c = es0.enter_context(nc.sbuf_tensor("gkv_c", [128, 4], F32))
    bkv_c = es0.enter_context(nc.sbuf_tensor("bkv_c", [128, 4], F32))
    stats6 = es0.enter_context(nc.sbuf_tensor("stats6", [128, 4, 24], F32))
    mv = es0.enter_context(nc.sbuf_tensor("mv", [128, 4, 2], F32))
    lnv = es0.enter_context(nc.sbuf_tensor("lnv", [128, 4], F32))
    rstd = es0.enter_context(nc.sbuf_tensor("rstd", [128, 4], F32))
    den = es0.enter_context(nc.sbuf_tensor("den", [128, 128], F32))
    rden = es0.enter_context(nc.sbuf_tensor("rden", [128, 128], F32))
    ones_b = es0.enter_context(nc.sbuf_tensor("ones_b", [1, 128], BF16))
    czero = es0.enter_context(nc.sbuf_tensor("czero", [128, 1], F32))
    ceps = es0.enter_context(nc.sbuf_tensor("ceps", [128, 1], F32))
    nc.const_aps.register(F32, 0.0, czero[:, :])
    nc.const_aps.register(F32, EPS, ceps[:, :])

    es_ac = ExitStack()
    cqnT = es_ac.enter_context(nc.sbuf_tensor("cqnT", [128, 4, TOWN], BF16))
    ckvnT = es_ac.enter_context(nc.sbuf_tensor("ckvnT", [128, 4, S], BF16))
    wuq_b = es_ac.enter_context(nc.sbuf_tensor("wuq_b", [128, 4, D], BF16))
    wukv_b = es_ac.enter_context(nc.sbuf_tensor("wukv_b", [128, 4, 2 * D], BF16))
    mask_sb = es_ac.enter_context(nc.sbuf_tensor("mask_sb", [128, NT, 512], BF16))

    es_a = ExitStack()
    xT_b = es_a.enter_context(nc.sbuf_tensor("xT_b", [128, 16, S], BF16))
    xTo_b = es_a.enter_context(nc.sbuf_tensor("xTo_b", [128, 16, TOWN], BF16))
    wd_b = es_a.enter_context(nc.sbuf_tensor("wd_b", [128, 16, LAT], BF16))
    stage = es_a.enter_context(nc.sbuf_tensor("stage", [128, 4, 512], BF16))
    ps_lat = es_a.enter_context(nc.psum_tensor("ps_lat", [128, 2048], F32))
    ps_T = es_a.enter_context(nc.psum_tensor("ps_T", [128, 4096], BF16))

    em("gpsimd", lambda e: e.dma_start(
        out=xT_b[:, :, :], in_=xT.rearrange("(c p) s -> p c s", p=128)), inc=sg, inc_k=16)
    em("gpsimd", lambda e: e.dma_start(
        out=xTo_b[:, :, :], in_=xTo.rearrange("(c p) s -> p c s", p=128)), inc=sg, inc_k=16)
    em("gpsimd", lambda e: e.dma_start(
        out=wd_b[:, :, :], in_=wdkv.rearrange("(c p) l -> p c l", p=128)), inc=sg, inc_k=16)
    v_ax = sg.n
    em("gpsimd", lambda e: e.dma_start(
        out=wuq_b[:, :, :], in_=wuq.rearrange("(c p) f -> p c f", p=128)), inc=sg, inc_k=16)
    for half in range(2):
        em("gpsimd", lambda e, half=half: e.dma_start(
            out=wukv_b[:, :, half * D:(half + 1) * D],
            in_=wukv.rearrange("(c p) f -> p c f", p=128)[:, :, half * D:(half + 1) * D]),
            inc=sg, inc_k=16)
    v_wu = sg.n

    em("sync", lambda e: e.dma_start(out=idb[:, :], in_=ident_b[:, :]), inc=ss, inc_k=16)
    em("sync", lambda e: e.dma_start(out=idf[:, :], in_=ident_f[:, :]), inc=ss, inc_k=16)
    em("sync", lambda e: e.dma_start(
        out=mask_sb[:, :, :], in_=masks.rearrange("j p c -> p j c")), inc=ss, inc_k=16)
    for dst, src in ((gq_c, qg), (bq_c, qb), (gkv_c, kvg), (bkv_c, kvb)):
        em("sync", lambda e, dst=dst, src=src: e.dma_start(
            out=dst[:, :], in_=src.rearrange("(c p) -> p c", p=128),
            allow_slow_non_contiguous=True), inc=ss, inc_k=16)
    v_consts = ss.n
    em("vector", lambda e: e.memset(ones_b[:, :], 1.0), inc=sdve)
    em("vector", lambda e: e.memset(czero[:, :], 0.0), inc=sdve)
    em("vector", lambda e: e.memset(ceps[:, :], EPS), inc=sdve)

    tiles = [("kv", i) for i in range(NKT)] + [("q", j) for j in range(NT)]
    v_apply = {}
    v_evT = {}
    v_tp = {}
    stA = {"mm_last": 0, "wdq_ld": v_ax}

    def a_front(idx):
        kind, i = tiles[idx]
        if idx == NKT:
            em("gpsimd", lambda e: e.dma_start(
                out=wd_b[:, :, :], in_=wdq.rearrange("(c p) l -> p c l", p=128)),
                waits=[(spe, stA["mm_last"])], inc=sg, inc_k=16)
            stA["wdq_ld"] = sg.n
        ph = idx % 4
        psl = ps_lat[:, ph * 512:(ph + 1) * 512]
        statw = xT_b if kind == "kv" else xTo_b
        waits = [(sg, (v_ax if kind == "kv" else stA["wdq_ld"]))]
        if idx >= 4:
            waits.append((sdve, v_apply[idx - 4]))
        for dc in range(16):
            em("tensor", lambda e, psl=psl, sw=statw, dc=dc, i=i: e.matmul(
                psl, sw[:, dc, i * 128:(i + 1) * 128], wd_b[:, dc, :],
                start=(dc == 0), stop=(dc == 15)),
                waits=waits if dc == 0 else (),
                inc=spe if dc == 15 else None)
        v_mm = spe.n
        stA["mm_last"] = v_mm
        em("vector", lambda e, ph=ph, psl=psl: e.bn_stats(stats6[:, ph, 0:6], psl),
           waits=[(spe, v_mm)], inc=sdve)
        v_bs = sdve.n
        em("vector", lambda e, ph=ph: e.bn_aggr(mv[:, ph, :], stats6[:, ph, 0:6]),
           waits=[(sdve, v_bs)], inc=sdve)
        v_stats = sdve.n
        em("scalar", lambda e, ph=ph: e.activation(
            lnv[:, ph:ph + 1], mv[:, ph, 1:2], AF.Ln, bias=EPS),
            waits=[(sdve, v_stats)], inc=sact)
        v_ln = sact.n
        em("scalar", lambda e, ph=ph: e.activation(
            rstd[:, ph:ph + 1], lnv[:, ph:ph + 1], AF.Exp, scale=-0.5),
            waits=[(sact, v_ln)], inc=sact)
        v_rstd = sact.n
        wap = [(sact, v_rstd)]
        if idx >= 4:
            wap.append((spe, v_tp[idx - 4]))
        em("vector", lambda e, ph=ph, psl=psl: e.tensor_scalar(
            stage[:, ph, :], psl, mv[:, ph, 0:1], rstd[:, ph:ph + 1],
            ALU.subtract, ALU.mult),
            waits=wap, inc=sdve)
        v_apply[idx] = sdve.n

    def a_back(idx):
        kind, i = tiles[idx]
        ph = idx % 4
        if kind == "kv":
            dstT, g_c, b_c = ckvnT, gkv_c, bkv_c
        else:
            dstT, g_c, b_c = cqnT, gq_c, bq_c
        twaits = [(sdve, v_apply[idx]), (ss, 16 * 2)]
        if idx >= 4:
            twaits.append((sdve, v_evT[idx - 4]))
        for lc in range(4):
            em("tensor", lambda e, ph=ph, lc=lc: e.transpose(
                ps_T[:, ph * 1024 + lc * 128: ph * 1024 + (lc + 1) * 128],
                stage[:, ph, lc * 128:(lc + 1) * 128], idb[:, :]),
                waits=twaits if lc == 0 else (),
                inc=spe if lc == 3 else None)
        v_tp[idx] = spe.n
        for lc in range(4):
            em("vector", lambda e, ph=ph, lc=lc, dstT=dstT, g_c=g_c, b_c=b_c, i=i:
               e.tensor_scalar(
                   dstT[:, lc, i * 128:(i + 1) * 128],
                   ps_T[:, ph * 1024 + lc * 128: ph * 1024 + (lc + 1) * 128],
                   g_c[:, lc:lc + 1], b_c[:, lc:lc + 1], ALU.mult, ALU.add),
               waits=[(spe, v_tp[idx])] if lc == 0 else (),
               inc=sdve if lc == 3 else None)
        v_evT[idx] = sdve.n

    for idx in range(len(tiles)):
        a_front(idx)
        if idx >= 1:
            a_back(idx - 1)
    a_back(len(tiles) - 1)
    fA = (spe.n, sdve.n, sact.n)
    v_ckv_done = v_evT[NKT - 1]
    v_cq_done = v_evT[len(tiles) - 1]
    es_a.close()

    def dump_and_finish(aps):
        fence = [(spe, spe.n), (sdve, sdve.n), (sact, sact.n)]
        n = 0
        for ap in aps:
            w = 1
            for dd in ap.shape[1:]:
                w *= dd
            np_ = ap.shape[0]
            em("gpsimd", lambda e, ap=ap, n=n, w=w, np_=np_: e.dma_start(
                out=y[n * 128:n * 128 + np_, 0:w].rearrange(
                    "p (a b) -> p a b", b=ap.shape[-1]) if len(ap.shape) == 3
                else y[n * 128:n * 128 + np_, 0:w],
                in_=ap),
                waits=fence if n == 0 else (), inc=sout, inc_k=16)
            n += 1
        em("gpsimd", lambda e, n=n: e.nop(), waits=[(sout, 16 * n)])
        finish(nc, r)
        return nc

    if TAP == "lat":
        return dump_and_finish(
            [cqnT[:, 0:2, :], cqnT[:, 2:4, :]] +
            [ckvnT[:, lc:lc + 1, :] for lc in range(4)])

    es_b = ExitStack()
    V_all = es_b.enter_context(nc.sbuf_tensor("V_all", [128, NKT, D], BF16))
    KT2 = es_b.enter_context(nc.sbuf_tensor("KT2", [128, 2, S], BF16))
    QT2 = es_b.enter_context(nc.sbuf_tensor("QT2", [128, 2, TOWN], BF16))
    Ptil = es_b.enter_context(nc.sbuf_tensor("Ptil", [128, 2, 2048], BF16))
    PT_buf = es_b.enter_context(nc.sbuf_tensor("PT_buf", [128, 2048], BF16))
    den4 = es_b.enter_context(nc.sbuf_tensor("den4", [128, 2, 4], F32))
    es_o = ExitStack()
    o_all = es_o.enter_context(nc.sbuf_tensor("o_all", [128, NT, D], BF16, side="right"))

    es_bp = ExitStack()
    ps_v = es_bp.enter_context(nc.psum_tensor("ps_v", [128, 4096], F32))
    v_vev = {}
    for kt in range(NKT):
        base = (kt % 2) * 2048
        w0 = [(sg, v_wu), (sdve, fA[1]), (sact, fA[2])]
        if kt >= 2:
            w0.append((sdve, v_vev[kt - 2]))
        for lc in range(4):
            for ft in range(4):
                em("tensor", lambda e, base=base, lc=lc, ft=ft, kt=kt: e.matmul(
                    ps_v[:, base + ft * 512: base + (ft + 1) * 512],
                    ckvnT[:, lc, kt * 128:(kt + 1) * 128],
                    wukv_b[:, lc, D + ft * 512: D + (ft + 1) * 512],
                    start=(lc == 0), stop=(lc == 3)),
                    waits=w0 if (lc == 0 and ft == 0) else (),
                    inc=spe if (lc == 3 and ft == 3) else None)
        v_pv = spe.n
        for ft in range(4):
            em("vector", lambda e, base=base, ft=ft, kt=kt: e.tensor_copy(
                V_all[:, kt, ft * 512:(ft + 1) * 512],
                ps_v[:, base + ft * 512: base + (ft + 1) * 512]),
                waits=[(spe, v_pv)] if ft == 0 else (),
                inc=sdve if ft == 3 else None)
        v_vev[kt] = sdve.n
    fB = (spe.n, sdve.n, sact.n)
    es_bp.close()

    es_cp = ExitStack()
    S_ps = es_cp.enter_context(nc.psum_tensor("S_ps", [128, 2048], F32))
    T_ps = es_cp.enter_context(nc.psum_tensor("T_ps", [128, 2048], BF16))
    okq_ps = es_cp.enter_context(nc.psum_tensor("okq_ps", [128, 1024], F32))
    o_ps = okq_ps[:, 0:128]
    kq_ps = okq_ps[:, 512:1024]

    v_kq_ready = {}
    v_kq_freed = {}
    st = {"exp": 0, "Tdone": 0, "oev": 0, "kq_evac": fB[1],
          "exp_bank": [0, 0, 0, 0], "Tchunk": {}, "exp_bank_u": {}}

    def emit_kq(h):
        jobs = [("k", t) for t in range(4)] + [("q", t) for t in range(2)]
        for (knd, t) in jobs:
            w0 = [(sdve, st["kq_evac"]), (sdve, v_ckv_done if knd == "k" else v_cq_done),
                  (sg, v_wu)]
            if h >= 2 and (h - 2) in v_kq_freed:
                w0.append((spe, v_kq_freed[h - 2]))
            for lc in range(4):
                if knd == "k":
                    em("tensor", lambda e, h=h, t=t, lc=lc: e.matmul(
                        kq_ps, wukv_b[:, lc, h * 128:(h + 1) * 128],
                        ckvnT[:, lc, t * 512:(t + 1) * 512],
                        start=(lc == 0), stop=(lc == 3)),
                        waits=w0 if lc == 0 else (),
                        inc=spe if lc == 3 else None)
                else:
                    em("tensor", lambda e, h=h, t=t, lc=lc: e.matmul(
                        kq_ps[:, 0:512], wuq_b[:, lc, h * 128:(h + 1) * 128],
                        cqnT[:, lc, t * 512:(t + 1) * 512],
                        start=(lc == 0), stop=(lc == 3)),
                        waits=w0 if lc == 0 else (),
                        inc=spe if lc == 3 else None)
            v_pk = spe.n
            if knd == "k":
                em("vector", lambda e, h=h, t=t: e.tensor_copy(
                    KT2[:, h % 2, t * 512:(t + 1) * 512], kq_ps),
                    waits=[(spe, v_pk)], inc=sdve)
            else:
                em("vector", lambda e, h=h, t=t: e.tensor_copy(
                    QT2[:, h % 2, t * 512:(t + 1) * 512], kq_ps[:, 0:512]),
                    waits=[(spe, v_pk)], inc=sdve)
            st["kq_evac"] = sdve.n
        v_kq_ready[h] = st["kq_evac"]

    emit_kq(0)
    for h in range(NH):
        if h + 1 < NH:
            emit_kq(h + 1)
        hb = h % 2
        for j in range(NT):
            u = h * NT + j
            up = u % 2
            nk = NK512[j]
            nkc = nk * 4
            v_sc_c = {}
            for c in range(nk):
                w0 = [(sact, st["exp_bank"][c])]
                if c == 0:
                    w0 += [(sdve, v_kq_ready[h]), (ss, 16 * 3)]
                em("tensor", lambda e, hb=hb, j=j, c=c, nk=nk: e.matmul(
                    S_ps[:, c * 512:(c + 1) * 512],
                    QT2[:, hb, j * 128:(j + 1) * 128],
                    KT2[:, hb, c * 512:(c + 1) * 512],
                    start=True, stop=(c < nk - 1)),
                    waits=w0,
                    inc=spe if c < nk - 1 else None)
                if c == nk - 1:
                    em("tensor", lambda e, j=j, nk=nk: e.matmul(
                        S_ps[:, (nk - 1) * 512: nk * 512],
                        idb[:, :], mask_sb[:, j, :], start=False, stop=True),
                        inc=spe)
                v_sc_c[c] = spe.n
            if j == NT - 1:
                v_kq_freed[h] = v_sc_c[nk - 1]
            for c in range(nk):
                wx = [(spe, v_sc_c[c]), (spe, st["Tchunk"].get((up, c), 0))]
                em("scalar", lambda e, c=c, up=up: e.activation(
                    Ptil[:, up, c * 512:(c + 1) * 512],
                    S_ps[:, c * 512:(c + 1) * 512], AF.Exp,
                    scale=ISQ, accum_out=den4[:, up, c:c + 1]),
                    waits=wx, inc=sact)
                st["exp_bank"][c] = sact.n
            v_exp_last = sact.n
            em("vector", lambda e, u=u, up=up, nk=nk: e.reduce_sum(
                den[:, u:u + 1], den4[:, up, 0:nk], mybir.AxisListType.X),
                waits=[(sact, v_exp_last)], inc=sdve)
            v_dn = sdve.n
            em("vector", lambda e, u=u: e.reciprocal(rden[:, u:u + 1], den[:, u:u + 1]),
               waits=[(sdve, v_dn)], inc=sdve)
            v_rdn = sdve.n
            ngr = (nkc + 3) // 4
            v_T = {}
            v_Tev = {}

            def emit_T(g, u=u, up=up, nkc=nkc, v_T=v_T, v_Tev=v_Tev, v_sc_c=v_sc_c):
                phT = g % 2
                kcs = list(range(g * 4, min((g + 1) * 4, nkc)))
                w1 = [(sact, st["exp_bank_u"][(u, g)])]
                if g >= 2:
                    w1.append((sdve, v_Tev[g - 2]))
                for n, kc in enumerate(kcs):
                    em("tensor", lambda e, phT=phT, n=n, kc=kc, up=up: e.transpose(
                        T_ps[:, phT * 1024 + n * 128: phT * 1024 + (n + 1) * 128],
                        Ptil[:, up, kc * 128:(kc + 1) * 128], idb[:, :]),
                        waits=w1 if n == 0 else (),
                        inc=spe if n == len(kcs) - 1 else None)
                v_T[g] = spe.n
                st["Tchunk"][(up, g)] = v_T[g]
                nn = len(kcs)
                em("vector", lambda e, phT=phT, g=g, nn=nn: e.tensor_copy(
                    PT_buf[:, g * 512: g * 512 + nn * 128],
                    T_ps[:, phT * 1024: phT * 1024 + nn * 128]),
                    waits=[(spe, v_T[g])], inc=sdve)
                v_Tev[g] = sdve.n

            def emit_AV(g, h=h, nkc=nkc, v_Tev=v_Tev):
                kcs = list(range(g * 4, min((g + 1) * 4, nkc)))
                w2 = [(sdve, v_Tev[g])]
                if g == 0:
                    w2.append((sdve, st["oev"]))
                for n, kc in enumerate(kcs):
                    em("tensor", lambda e, kc=kc, h=h, nkc=nkc: e.matmul(
                        o_ps, PT_buf[:, kc * 128:(kc + 1) * 128],
                        V_all[:, kc, h * 128:(h + 1) * 128],
                        start=(kc == 0), stop=(kc == nkc - 1)),
                        waits=w2 if n == 0 else (),
                        inc=spe if kc == nkc - 1 else None)

            for g in range(ngr):
                st["exp_bank_u"][(u, g)] = st["exp_bank"][g]
            emit_T(0)
            for g in range(1, ngr):
                emit_T(g)
                emit_AV(g - 1)
            emit_AV(ngr - 1)
            v_av = spe.n
            em("vector", lambda e, j=j, h=h, u=u: e.tensor_scalar(
                o_all[:, j, h * 128:(h + 1) * 128], o_ps,
                rden[:, u:u + 1], None, ALU.mult),
                waits=[(spe, v_av), (sdve, v_rdn)], inc=sdve)
            st["oev"] = sdve.n
    fC = (spe.n, sdve.n, sact.n)
    if TAP == "o":
        return dump_and_finish([o_all[:, j, :] for j in range(NT)])
    if TAP == "V":
        return dump_and_finish(
            [V_all[:, kt, :] for kt in range(4)] +
            [KT2[:, 1, :], QT2[:, 1, :], Ptil[:, :], den[:, :]])
    es_cp.close()
    es_b.close()
    es_ac.close()

    es_x1T = ExitStack()
    x1T = es_x1T.enter_context(nc.sbuf_tensor("x1T", [128, 16, TOWN], BF16))
    es_d = ExitStack()
    oT_all = es_d.enter_context(nc.sbuf_tensor("oT_all", [128, 16, TOWN], BF16))

    es_d0p = ExitStack()
    ps_Td = es_d0p.enter_context(nc.psum_tensor("ps_Td", [128, 2048], BF16))
    v_tev = {}
    gidx = 0
    for tt in range(NT):
        for hg in range(4):
            phT = gidx % 2
            w0 = [(sdve, fC[1]), (sact, fC[2])]
            if gidx >= 2:
                w0.append((sdve, v_tev[gidx - 2]))
            for n in range(4):
                hc = hg * 4 + n
                em("tensor", lambda e, phT=phT, n=n, tt=tt, hc=hc: e.transpose(
                    ps_Td[:, phT * 1024 + n * 128: phT * 1024 + (n + 1) * 128],
                    o_all[:, tt, hc * 128:(hc + 1) * 128], idb[:, :]),
                    waits=w0 if n == 0 else (),
                    inc=spe if n == 3 else None)
            v_tp = spe.n
            em("vector", lambda e, phT=phT, tt=tt, hg=hg: e.tensor_copy(
                oT_all[:, hg * 4:(hg + 1) * 4, tt * 128:(tt + 1) * 128],
                ps_Td[:, phT * 1024: phT * 1024 + 512].rearrange(
                    "p (a b) -> p a b", a=4)),
                waits=[(spe, v_tp)], inc=sdve)
            v_tev[gidx] = sdve.n
            gidx += 1
    fD0 = (spe.n, sdve.n)
    es_d0p.close()
    if TAP == "oT":
        return dump_and_finish([oT_all[:, ic, :] for ic in range(8)])
    es_o.close()

    woTc = es_d.enter_context(nc.sbuf_tensor("woTc", [128, 2, 16, 512], BF16))
    xo_sl = es_d.enter_context(nc.sbuf_tensor("xo_sl", [128, NT, 512], F32))
    x1p_sl = es_d.enter_context(nc.sbuf_tensor("x1p_sl", [128, 4, 512], F32))
    stats6d = es_d.enter_context(nc.sbuf_tensor("stats6d", [128, NT, 24], F32))
    es_d1p = ExitStack()
    ps_ao = es_d1p.enter_context(nc.psum_tensor("ps_ao", [128, 4096], F32))

    v_wo_used = {}
    v_jt_evacs = {}
    v_x1p_spill = []
    for jt in range(4):
        wld = [(spe, fD0[0]), (sdve, fD0[1])]
        if jt >= 2:
            wld.append((spe, v_wo_used[jt - 2]))
        em("gpsimd", lambda e, jt=jt: e.dma_start(
            out=woTc[:, jt % 2, :, :],
            in_=woT.rearrange("(c p) f -> p c f", p=128)[:, :, jt * 512:(jt + 1) * 512]),
            waits=wld, inc=sg, inc_k=16)
        v_wold = sg.n
        wxo = ([(sdve, v_jt_evacs[jt - 1])] if jt >= 1
               else [(spe, fD0[0]), (sdve, fD0[1])])
        for tt in range(NT):
            em("sync", lambda e, jt=jt, tt=tt: e.dma_start(
                out=xo_sl[:, tt, :],
                in_=x_own[tt * 128:(tt + 1) * 128, jt * 512:(jt + 1) * 512]),
                waits=wxo if tt == 0 else (), inc=ss, inc_k=16)
        v_xold = ss.n
        w0 = [(sg, v_wold), (spe, fD0[0])]
        if jt >= 1:
            w0.append((sdve, v_jt_evacs[jt - 1]))
        for ic in range(16):
            for tt in range(NT):
                em("tensor", lambda e, jt=jt, ic=ic, tt=tt: e.matmul(
                    ps_ao[:, tt * 512:(tt + 1) * 512],
                    oT_all[:, ic, tt * 128:(tt + 1) * 128],
                    woTc[:, jt % 2, ic, :],
                    start=(ic == 0), stop=(ic == 15)),
                    waits=w0 if (ic == 0 and tt == 0) else (),
                    inc=spe if (ic == 15 and tt == NT - 1) else None)
        v_wo_used[jt] = spe.n
        for tt in range(NT):
            k = jt * NT + tt
            wev = [(spe, v_wo_used[jt]), (ss, v_xold)]
            if k >= 4:
                wev.append((sspl, 16 * (k - 3)))
            em("vector", lambda e, jt=jt, tt=tt, k=k: e.tensor_tensor(
                x1p_sl[:, k % 4, :], ps_ao[:, tt * 512:(tt + 1) * 512],
                xo_sl[:, tt, :], ALU.add),
                waits=wev, inc=sdve)
            v_tt = sdve.n
            em("vector", lambda e, jt=jt, tt=tt, k=k: e.bn_stats(
                stats6d[:, tt, jt * 6:(jt + 1) * 6], x1p_sl[:, k % 4, :]),
                waits=[(sdve, v_tt)], inc=sdve)
            v_bs = sdve.n
            em("sync", lambda e, jt=jt, tt=tt, k=k: e.dma_start(
                out=x1pre_sp[tt * 128:(tt + 1) * 128, jt * 512:(jt + 1) * 512],
                in_=x1p_sl[:, k % 4, :]),
                waits=[(sdve, v_bs)], inc=sspl, inc_k=16)
            v_x1p_spill.append(sspl.n)
        v_jt_evacs[jt] = sdve.n
        if TAP == "d1" and jt == 0:
            return dump_and_finish([
                x1p_sl[:, 0:4, :], xo_sl[:, 0:4, :], xo_sl[:, 4:8, :],
                woTc[:, 0, 0:4, :], woTc[:, 0, 4:8, :], woTc[:, 0, 8:12, :],
                woTc[:, 0, 12:16, :]])
    fD1 = (spe.n, sdve.n, sact.n)
    es_d1p.close()

    if TAP == "x1pre":
        em("gpsimd", lambda e: e.dma_start(out=y[:, :], in_=x1pre_sp[:, :]),
           waits=[(sspl, 16 * 32)], inc=sout, inc_k=16)
        em("gpsimd", lambda e: e.dma_start(
            out=y[0:128, 0:192].rearrange("p (a b) -> p a b", b=24),
            in_=stats6d[:, :, :]),
            waits=[(sdve, fD1[1]), (sout, 16)], inc=sout, inc_k=16)
        em("gpsimd", lambda e: e.nop(), waits=[(sout, 32)])
        finish(nc, r)
        return nc

    es_d2 = ExitStack()
    x1rp = es_d2.enter_context(nc.sbuf_tensor("x1rp", [128, 2, 2048], F32))
    x1row = es_d2.enter_context(nc.sbuf_tensor("x1row", [128, 2, 2048], F32))
    es_d2p = ExitStack()
    ps_Tf = es_d2p.enter_context(nc.psum_tensor("ps_Tf", [128, 1024], F32))

    v_x1T = {}
    v_x1sp = {}
    st2 = {"apply": 0, "rl_free": 0, "tfev": [], "tf_last": {}}
    for tt in range(NT):
        ph = tt % 2
        wrl = [(sspl, 16 * 32)]
        if tt >= 2:
            wrl.append((sdve, st2["rl_free"]))
        em("sync", lambda e, tt=tt, ph=ph: e.dma_start(
            out=x1rp[:, ph, :], in_=x1pre_sp[tt * 128:(tt + 1) * 128, :]),
            waits=wrl, inc=ss, inc_k=16)
        v_rl = ss.n
        wag = [(sdve, fD1[1])] if tt == 0 else []
        em("vector", lambda e, tt=tt, ph=ph: e.bn_aggr(mv[:, ph, :], stats6d[:, tt, :]),
           waits=wag, inc=sdve)
        v_ag = sdve.n
        em("scalar", lambda e, ph=ph: e.activation(
            lnv[:, ph:ph + 1], mv[:, ph, 1:2], AF.Ln, bias=EPS),
            waits=[(sdve, v_ag)], inc=sact)
        v_ln2 = sact.n
        em("scalar", lambda e, ph=ph: e.activation(
            rstd[:, ph:ph + 1], lnv[:, ph:ph + 1], AF.Exp, scale=-0.5),
            waits=[(sact, v_ln2)], inc=sact)
        v_rs = sact.n
        wapp = [(sact, v_rs), (ss, v_rl)]
        if tt >= 2:
            if TAP == "x1":
                wapp.append((sout, 16 * (tt - 1)))
            else:
                wapp.append((sspl, 16 * (31 + tt)))
            wapp.append((spe, st2["tf_last"][tt - 2]))
        em("vector", lambda e, ph=ph: e.tensor_scalar(
            x1row[:, ph, :], x1rp[:, ph, :], mv[:, ph, 0:1], rstd[:, ph:ph + 1],
            ALU.subtract, ALU.mult),
            waits=wapp, inc=sdve)
        st2["rl_free"] = sdve.n
        v_app = sdve.n
        em("sync", lambda e, tt=tt, ph=ph: e.dma_start(
            out=(y if TAP == "x1" else x1_sp)[tt * 128:(tt + 1) * 128, :],
            in_=x1row[:, ph, :]),
            waits=[(sdve, v_app)], inc=(sout if TAP == "x1" else sspl), inc_k=16)
        v_x1sp[tt] = sspl.n
        for dg in range(4):
            phT = (tt * 4 + dg) % 2
            w1 = [(sdve, v_app)]
            if len(st2["tfev"]) >= 2:
                w1.append((sdve, st2["tfev"][-2]))
            for n in range(4):
                dc = dg * 4 + n
                em("tensor", lambda e, phT=phT, n=n, ph=ph, dc=dc: e.transpose(
                    ps_Tf[:, phT * 512 + n * 128: phT * 512 + (n + 1) * 128],
                    x1row[:, ph, dc * 128:(dc + 1) * 128], idf[:, :]),
                    waits=w1 if n == 0 else (),
                    inc=spe if n == 3 else None)
            v_tf = spe.n
            em("vector", lambda e, phT=phT, tt=tt, dg=dg: e.tensor_copy(
                x1T[:, dg * 4:(dg + 1) * 4, tt * 128:(tt + 1) * 128],
                ps_Tf[:, phT * 512: phT * 512 + 512].rearrange(
                    "p (a b) -> p a b", a=4)),
                waits=[(spe, v_tf)], inc=sdve)
            st2["tfev"].append(sdve.n)
            st2["tf_last"][tt] = v_tf
        v_x1T[tt] = sdve.n
    fD2 = (spe.n, sdve.n, sact.n)
    es_d2p.close()
    es_d2.close()
    es_d.close()

    if TAP == "x1":
        em("sync", lambda e: e.nop(), waits=[(sout, 16 * NT)])
        finish(nc, r)
        return nc

    es_e1 = ExitStack()
    w1q_b = es_e1.enter_context(nc.sbuf_tensor("w1q_b", [128, 2, 16, 256], BF16))
    w3q_b = es_e1.enter_context(nc.sbuf_tensor("w3q_b", [128, 2, 16, 256], BF16))
    w2c_b = es_e1.enter_context(nc.sbuf_tensor("w2c_b", [128, 2, 8, 512], BF16))
    wgT_b = es_e1.enter_context(nc.sbuf_tensor("wgT_b", [128, 16, 8], BF16))
    bias_sb = es_e1.enter_context(nc.sbuf_tensor("bias_sb", [128, 8], F32))

    em("gpsimd", lambda e: e.dma_start(
        out=wgT_b[:, :, :], in_=wgT.rearrange("(c p) n -> p c n", p=128)),
        waits=[(spe, fD2[0]), (sdve, fD2[1])], inc=sg, inc_k=16)
    em("gpsimd", lambda e: e.dma_start(out=sel8_b[:, :, :], in_=sel8[:, :, :]),
       inc=sg, inc_k=16)
    v_wg = sg.n
    em("sync", lambda e: e.dma_start(out=bias_sb[:, :], in_=bias_rep[:, :]),
       waits=[(sdve, fD2[1])], inc=ss, inc_k=16)
    v_bias = ss.n

    rw_all = es_e1.enter_context(nc.sbuf_tensor("rw_all", [128, NT, 8], F32))
    ew_all = es_e1.enter_context(nc.sbuf_tensor("ew_all", [128, NT, 8], F32))
    t8 = es_e1.enter_context(nc.sbuf_tensor("t8", [128, 5, 8], F32))
    t1c = es_e1.enter_context(nc.sbuf_tensor("t1c", [128, 4], F32))
    ewT8 = es_e1.enter_context(nc.sbuf_tensor("ewT8", [8, TOWN], BF16))
    sel8_b = es_e1.enter_context(nc.sbuf_tensor("sel8_b", [8, 8, 128], BF16))
    ew_bc = es_e1.enter_context(nc.sbuf_tensor("ew_bc", [128, 2, TOWN], BF16))
    h1s = es_e1.enter_context(nc.sbuf_tensor("h1s", [128, FFN], F32))
    g3 = es_e1.enter_context(nc.sbuf_tensor("g3", [128, FFN], F32))
    G = es_e1.enter_context(nc.sbuf_tensor("G", [128, 16, TOWN], BF16))
    x2acc = es_e1.enter_context(nc.sbuf_tensor("x2acc", [128, NT, D], F32))
    x1rl = es_e1.enter_context(nc.sbuf_tensor("x1rl", [128, 2, 512], F32))
    ytile = es_e1.enter_context(nc.sbuf_tensor("ytile", [128, D], F32))
    acc_tmp = x1rl[:, 0, :]

    es_ep = ExitStack()
    ps_e1 = es_ep.enter_context(nc.psum_tensor("ps_e1", [128, 2048], F32))
    ps_e2 = es_ep.enter_context(nc.psum_tensor("ps_e2", [128, 2048], F32))

    v_sig = {}
    for tt in range(NT):
        w0 = [(sdve, v_x1T[NT - 1]), (sg, v_wg)]
        if tt >= 1:
            w0.append((sact, v_sig[tt - 1]))
        for dc in range(16):
            em("tensor", lambda e, tt=tt, dc=dc: e.matmul(
                ps_e1[:, 0:8], x1T[:, dc, tt * 128:(tt + 1) * 128], wgT_b[:, dc, :],
                start=(dc == 0), stop=(dc == 15)),
                waits=w0 if dc == 0 else (),
                inc=spe if dc == 15 else None)
        v_rl = spe.n
        em("scalar", lambda e, tt=tt: e.activation(
            rw_all[:, tt, :], ps_e1[:, 0:8], AF.Sigmoid),
            waits=[(spe, v_rl)], inc=sact)
        v_sig[tt] = sact.n
    v_ew = {}
    for tt in range(NT):
        steps = [
            lambda e, tt=tt: e.tensor_tensor(
                t8[:, 0, :], rw_all[:, tt, :], bias_sb[:, :], ALU.add),
            lambda e: e.reduce_max(t1c[:, 0:1], t8[:, 0, :], mybir.AxisListType.X),
            lambda e: e.tensor_scalar(
                t8[:, 1, :], t8[:, 0, :], t1c[:, 0:1], None, ALU.is_equal),
            lambda e: e.scalar_tensor_tensor(
                t8[:, 2, :], t8[:, 1, :], NEG, t8[:, 0, :], ALU.mult, ALU.add),
            lambda e: e.reduce_max(t1c[:, 1:2], t8[:, 2, :], mybir.AxisListType.X),
            lambda e: e.tensor_scalar(
                t8[:, 3, :], t8[:, 2, :], t1c[:, 1:2], None, ALU.is_equal),
            lambda e: e.tensor_tensor(
                t8[:, 4, :], t8[:, 1, :], t8[:, 3, :], ALU.add),
            lambda e, tt=tt: e.tensor_tensor(
                t8[:, 4, :], rw_all[:, tt, :], t8[:, 4, :], ALU.mult),
            lambda e: e.reduce_sum(t1c[:, 2:3], t8[:, 4, :], mybir.AxisListType.X),
            lambda e: e.reciprocal(t1c[:, 3:4], t1c[:, 2:3]),
            lambda e, tt=tt: e.tensor_scalar(
                ew_all[:, tt, :], t8[:, 4, :], t1c[:, 3:4], None, ALU.mult),
        ]
        for si, fn in enumerate(steps):
            w = [(sact, v_sig[tt]), (ss, v_bias)] if si == 0 else [(sdve, sdve.n)]
            em("vector", fn, waits=w, inc=sdve)
        v_ew[tt] = sdve.n
    v_ew_done = sdve.n

    prev_tev = 0
    for tt in range(NT):
        w0 = [(sdve, v_ew[tt])]
        if prev_tev:
            w0.append((sdve, prev_tev))
        em("tensor", lambda e, tt=tt: e.transpose(
            ps_e2[0:8, 0:128], ew_all[:, tt, :], idf[:, :]),
            waits=w0, inc=spe)
        v_t = spe.n
        em("vector", lambda e, tt=tt: e.tensor_copy(
            ewT8[:, tt * 128:(tt + 1) * 128], ps_e2[0:8, 0:128]),
            waits=[(spe, v_t)], inc=sdve)
        prev_tev = sdve.n
    v_ewT_all = sdve.n
    if TAP == "ewt":
        return dump_and_finish([ewT8[:, :], ew_all[:, :, :], rw_all[:, :, :]])

    v_w13 = {}
    v_w13_used = {}
    v_w2 = {}
    v_w2_used = {}

    def load_w13(e_, q):
        kq = e_ * 4 + q
        wld = [(spe, fD2[0]), (sdve, fD2[1])]
        if kq >= 2:
            wld.append((spe, v_w13_used.get(kq - 2, 0)))
        em("gpsimd", lambda e, e_=e_, q=q: e.dma_start(
            out=w1q_b[:, q % 2, :, :],
            in_=w1T[e_].rearrange("(c p) f -> p c f", p=128)[:, :, q * 256:(q + 1) * 256]),
            waits=wld, inc=sg, inc_k=16)
        em("gpsimd", lambda e, e_=e_, q=q: e.dma_start(
            out=w3q_b[:, q % 2, :, :],
            in_=w3T[e_].rearrange("(c p) f -> p c f", p=128)[:, :, q * 256:(q + 1) * 256]),
            inc=sg, inc_k=16)
        v_w13[kq] = sg.n

    def load_w2(pair, ht, e2):
        e_ = pair * 2 + e2
        kc = pair * 8 + ht * 2 + e2
        wld = [(spe, fD2[0]), (sdve, fD2[1])]
        if kc >= 2:
            wld.append((spe, v_w2_used.get(kc - 2, 0)))
        em("gpsimd", lambda e, e_=e_, ht=ht, kc=kc: e.dma_start(
            out=w2c_b[:, kc % 2, :, :],
            in_=w2T[e_].rearrange("(c p) f -> p c f", p=128)[:, :, ht * 512:(ht + 1) * 512]),
            waits=wld, inc=sg, inc_k=16)
        v_w2[kc] = sg.n

    v_x1rl = {}
    v_x1rl_evac = {}

    def load_x1rl(k):
        if k == 0:
            wld = [(sspl, 16 * 40), (sdve, fD2[1]), (spe, fD2[0])]
        else:
            wld = [(sdve, v_x1rl_evac[k - 2])] if k >= 2 else []
        em("sync", lambda e, k=k: e.dma_start(
            out=x1rl[:, k % 2, :],
            in_=x1_sp[(k % 8) * 128:((k % 8) + 1) * 128,
                      (k // 8) * 512:((k // 8) + 1) * 512]),
            waits=wld, inc=ss, inc_k=16)
        v_x1rl[k] = ss.n

    for k in range(2):
        load_x1rl(k)

    st_e = {"set_last": {}, "psum_free": 0, "w2_mm_last": 0, "g_prev": 0}
    v_pair_evacs = {}
    for pair in range(4):
        v_prev_bc = v_ewT_all
        for e2 in range(2):
            e_ = pair * 2 + e2
            for t5 in range(2):
                w0 = [(sdve, v_prev_bc), (sg, v_wg)]
                if st_e["psum_free"]:
                    w0.append((sdve, st_e["psum_free"]))
                em("tensor", lambda e, e_=e_, t5=t5: e.matmul(
                    ps_e2[:, 0:512], sel8_b[:, e_, :],
                    ewT8[:, t5 * 512:(t5 + 1) * 512],
                    start=True, stop=True),
                    waits=w0, inc=spe)
                v_b = spe.n
                em("vector", lambda e, e2=e2, t5=t5: e.tensor_copy(
                    ew_bc[:, e2, t5 * 512:(t5 + 1) * 512], ps_e2[:, 0:512]),
                    waits=[(spe, v_b)], inc=sdve)
                v_prev_bc = sdve.n
        v_bc_done = sdve.n
        if pair == 0:
            load_w13(0, 0)
            load_w13(0, 1)
        for e2 in range(2):
            e_ = pair * 2 + e2
            for fc in range(8):
                q = fc // 2
                kq = e_ * 4 + q
                set_ = fc % 2
                h1ps = ps_e1[:, set_ * 1024:(set_ + 1) * 1024]
                h3ps = ps_e2[:, set_ * 1024:(set_ + 1) * 1024]
                w0 = [(sg, v_w13[kq]), (sdve, v_bc_done)]
                if set_ in st_e["set_last"]:
                    w0.append((sdve, st_e["set_last"][set_]))
                if st_e["psum_free"]:
                    w0.append((sdve, st_e["psum_free"]))
                for dc in range(16):
                    for t5 in range(2):
                        em("tensor", lambda e, q=q, dc=dc, fc=fc, t5=t5, h1ps=h1ps:
                           e.matmul(
                               h1ps[:, t5 * 512:(t5 + 1) * 512],
                               w1q_b[:, q % 2, dc,
                                     (fc % 2) * 128:(fc % 2 + 1) * 128],
                               x1T[:, dc, t5 * 512:(t5 + 1) * 512],
                               start=(dc == 0), stop=(dc == 15)),
                           waits=w0 if (dc == 0 and t5 == 0) else (),
                           inc=spe if (dc == 15 and t5 == 1) else None)
                v_h1 = spe.n
                for dc in range(16):
                    for t5 in range(2):
                        em("tensor", lambda e, q=q, dc=dc, fc=fc, t5=t5, h3ps=h3ps:
                           e.matmul(
                               h3ps[:, t5 * 512:(t5 + 1) * 512],
                               w3q_b[:, q % 2, dc,
                                     (fc % 2) * 128:(fc % 2 + 1) * 128],
                               x1T[:, dc, t5 * 512:(t5 + 1) * 512],
                               start=(dc == 0), stop=(dc == 15)),
                           inc=spe if (dc == 15 and t5 == 1) else None)
                v_h3 = spe.n
                v_w13_used[kq] = v_h3
                if fc % 2 == 1 and kq + 2 < 32:
                    load_w13((kq + 2) // 4, (kq + 2) % 4)
                wsi = [(spe, v_h1)]
                if st_e["g_prev"]:
                    wsi.append((sdve, st_e["g_prev"]))
                em("scalar", lambda e, h1ps=h1ps: e.activation(
                    h1s[:, :], h1ps, AF.Silu),
                    waits=wsi, inc=sact)
                v_si = sact.n
                em("vector", lambda e, h3ps=h3ps, e2=e2: e.scalar_tensor_tensor(
                    g3[:, :], h3ps, 1.0, ew_bc[:, e2, :], ALU.mult, ALU.mult),
                    waits=[(spe, v_h3), (sdve, v_bc_done)], inc=sdve)
                v_g3 = sdve.n
                em("vector", lambda e, e2=e2, fc=fc: e.tensor_tensor(
                    G[:, e2 * 8 + fc, :], h1s[:, :], g3[:, :], ALU.mult),
                    waits=[(sact, v_si), (sdve, v_g3)], inc=sdve)
                st_e["set_last"][set_] = sdve.n
                st_e["g_prev"] = sdve.n
                if TAP == "h1p%d" % pair and e2 == 0 and fc == 0:
                    return dump_and_finish([h1s[:, :], g3[:, :], G[:, 0, :],
                                            ew_bc[:, 0, :]])
        v_G_done = sdve.n
        if TAP == "g%d" % pair:
            return dump_and_finish([ew_bc[:, 0, :], ew_bc[:, 1, :],
                                    ewT8[:, :],
                                    G[:, 0, :], G[:, 15, :]])
        load_w2(pair, 0, 0)
        load_w2(pair, 0, 1)
        v_evac_tt = {}
        for ht in range(4):
            for e2 in range(2):
                kc = pair * 8 + ht * 2 + e2
                w0 = [(sg, v_w2[kc]), (sdve, v_G_done)]
                if ht == 0 and st_e["psum_free"]:
                    w0.append((sdve, st_e["set_last"][0]))
                for fc in range(8):
                    for tt in range(NT):
                        psb = (ps_e1 if tt < 4 else ps_e2)[
                            :, (tt % 4) * 512:((tt % 4) + 1) * 512]
                        wm = list(w0) if (fc == 0 and tt == 0) else []
                        if ht >= 1 and e2 == 0 and fc == 0:
                            wm.append((sdve, v_evac_tt[(ht - 1, tt)]))
                        em("tensor", lambda e, kc=kc, e2=e2, fc=fc, tt=tt, psb=psb:
                           e.matmul(
                               psb, G[:, e2 * 8 + fc, tt * 128:(tt + 1) * 128],
                               w2c_b[:, kc % 2, fc, :],
                               start=(e2 == 0 and fc == 0),
                               stop=(e2 == 1 and fc == 7)),
                           waits=wm,
                           inc=spe if (fc == 7 and tt == NT - 1) else None)
                v_w2_used[kc] = spe.n
            v_wmm = spe.n
            if ht + 1 < 4:
                load_w2(pair, ht + 1, 0)
                load_w2(pair, ht + 1, 1)
            for tt in range(NT):
                psb = (ps_e1 if tt < 4 else ps_e2)[
                    :, (tt % 4) * 512:((tt % 4) + 1) * 512]
                k = ht * NT + tt
                if pair == 0:
                    em("vector", lambda e, tt=tt, ht=ht, psb=psb, k=k: e.tensor_tensor(
                        x2acc[:, tt, ht * 512:(ht + 1) * 512], psb,
                        x1rl[:, k % 2, :], ALU.add),
                        waits=[(spe, v_wmm), (ss, v_x1rl[k])],
                        inc=sdve)
                    v_x1rl_evac[k] = sdve.n
                    if k + 2 < 32:
                        load_x1rl(k + 2)
                else:
                    em("vector", lambda e, psb=psb, tt=tt, ht=ht: e.tensor_tensor(
                        acc_tmp, psb,
                        x2acc[:, tt, ht * 512:(ht + 1) * 512], ALU.add),
                        waits=[(spe, v_wmm)] if tt == 0 else (), inc=sdve)
                    v_at = sdve.n
                    em("vector", lambda e, tt=tt, ht=ht: e.tensor_copy(
                        x2acc[:, tt, ht * 512:(ht + 1) * 512], acc_tmp),
                        waits=[(sdve, v_at)], inc=sdve)
                v_evac_tt[(ht, tt)] = sdve.n
            v_pair_evacs[(pair, ht)] = sdve.n
        st_e["psum_free"] = v_pair_evacs[(pair, 3)]
        st_e["w2_mm_last"] = spe.n
        st_e["set_last"] = {}
        if TAP == "x2p%d" % pair:
            return dump_and_finish([x2acc[:, tt, :] for tt in range(NT)])
    v_ffn_done = sdve.n

    if TAP == "x2":
        return dump_and_finish([x2acc[:, tt, :] for tt in range(NT)])
    if TAP == "ew":
        return dump_and_finish([rw_all[:, :, :], ew_all[:, :, :],
                                ew_bc[:, 0, :], ew_bc[:, 1, :],
                                G[:, 0, :], G[:, 15, :]])

    for tt in range(NT):
        ph = tt % 2
        w0 = [(sdve, v_ffn_done)]
        for c in range(4):
            em("vector", lambda e, tt=tt, ph=ph, c=c: e.bn_stats(
                stats6[:, ph, c * 6:(c + 1) * 6],
                x2acc[:, tt, c * 512:(c + 1) * 512]),
                waits=w0 if c == 0 else (),
                inc=sdve if c == 3 else None)
        v_bs4 = sdve.n
        em("vector", lambda e, ph=ph: e.bn_aggr(mv[:, ph, :], stats6[:, ph, :]),
           waits=[(sdve, v_bs4)], inc=sdve)
        v_ag = sdve.n
        em("scalar", lambda e, ph=ph: e.activation(
            lnv[:, ph:ph + 1], mv[:, ph, 1:2], AF.Ln, bias=EPS),
            waits=[(sdve, v_ag)], inc=sact)
        v_ln2 = sact.n
        em("scalar", lambda e, ph=ph: e.activation(
            rstd[:, ph:ph + 1], lnv[:, ph:ph + 1], AF.Exp, scale=-0.5),
            waits=[(sact, v_ln2)], inc=sact)
        v_rs = sact.n
        wapp = [(sact, v_rs)]
        if tt >= 1:
            wapp.append((sout, 16 * tt))
        em("vector", lambda e, tt=tt, ph=ph: e.tensor_scalar(
            ytile[:, :], x2acc[:, tt, :], mv[:, ph, 0:1], rstd[:, ph:ph + 1],
            ALU.subtract, ALU.mult),
            waits=wapp, inc=sdve)
        v_yt = sdve.n
        em("sync", lambda e, tt=tt: e.dma_start(
            out=y[tt * 128:(tt + 1) * 128, :], in_=ytile[:, :]),
            waits=[(sdve, v_yt)], inc=sout, inc_k=16)
    em("sync", lambda e: e.nop(), waits=[(sout, 16 * NT)])

    finish(nc, r)
    return nc


def finish(nc, r):
    with nc.Block() as blk:
        @blk.tensor
        def _(e):
            for f in r.ops["tensor"]:
                f(e)

        @blk.vector
        def _(e):
            for f in r.ops["vector"]:
                f(e)

        @blk.scalar
        def _(e):
            for f in r.ops["scalar"]:
                f(e)

        @blk.sync
        def _(e):
            for f in r.ops["sync"]:
                f(e)

        @blk.gpsimd
        def _(e):
            for f in r.ops["gpsimd"]:
                f(e)


def _host_inputs(inputs):
    f32 = np.float32
    x = np.asarray(inputs["L0_attn_x"], f32)
    wdq = np.asarray(inputs["L0_attn_W_dq"], f32)
    wuq = np.asarray(inputs["L0_attn_W_uq"], f32)
    wdkv = np.asarray(inputs["L0_attn_W_dkv"], f32)
    wukv = np.asarray(inputs["L0_attn_W_ukv"], f32)
    wo = np.asarray(inputs["L0_attn_W_o"], f32)
    qg = np.asarray(inputs["L0_attn_q_ln_g"], f32)
    qb = np.asarray(inputs["L0_attn_q_ln_b"], f32)
    kvg = np.asarray(inputs["L0_attn_kv_ln_g"], f32)
    kvb = np.asarray(inputs["L0_attn_kv_ln_b"], f32)
    wg = np.asarray(inputs["L0_ffn_Wg"], f32)
    w1 = np.asarray(inputs["L0_ffn_W1"], f32)
    w2 = np.asarray(inputs["L0_ffn_W2"], f32)
    w3 = np.asarray(inputs["L0_ffn_W3"], f32)
    fb = np.asarray(inputs["L0_ffn_bias"], f32)

    bf = ml_dtypes.bfloat16
    ident_b = np.eye(128, dtype=bf)
    ident_f = np.eye(128, dtype=f32)
    woT = np.ascontiguousarray(wo.T)
    w1T = np.ascontiguousarray(w1.transpose(0, 2, 1))
    w3T = np.ascontiguousarray(w3.transpose(0, 2, 1))
    w2T = np.ascontiguousarray(w2.transpose(0, 2, 1))
    wgT = np.ascontiguousarray(wg.T)
    bias_rep = np.ascontiguousarray(np.tile(fb[None, :], (128, 1)))
    sel8 = np.zeros((8, 8, 128), dtype=bf)
    for e in range(8):
        sel8[e, e, :] = 1.0

    shared = dict(wdq=wdq, wuq=wuq, wdkv=wdkv, wukv=wukv, woT=woT,
                  qg=qg, qb=qb, kvg=kvg, kvb=kvb,
                  w1T=w1T, w3T=w3T, w2T=w2T, wgT=wgT, bias_rep=bias_rep,
                  sel8=sel8, ident_b=ident_b, ident_f=ident_f)

    mk = {}
    for p in range(2):
        m = np.zeros((NT, 128, 512), dtype=np.float32)
        for j in range(NT):
            Q = 2 * j + p
            nk = NK512[j]
            k0 = (nk - 1) * 512
            for rr in range(128):
                qi = Q * 128 + rr
                c_ok = qi - k0
                if c_ok < 511:
                    m[j, rr, max(c_ok + 1, 0):] = NEG
        mk[p] = m.astype(bf)

    in_maps = []
    owners = []
    for c in range(8):
        b, p = c // 2, c % 2
        rows = np.concatenate([np.arange(128) + (2 * j + p) * 128 for j in range(NT)])
        owners.append((b, rows))
        xb = x[b]
        d = dict(shared)
        d["xT"] = np.ascontiguousarray(xb.T)
        d["xTo"] = np.ascontiguousarray(xb[rows].T)
        d["x_own"] = np.ascontiguousarray(xb[rows])
        d["masks"] = mk[p]
        in_maps.append(d)
    return in_maps, owners


_CACHED = {}


def kernel(**inputs):
    in_maps, owners = _host_inputs(inputs)
    if "nc" not in _CACHED:
        _CACHED["nc"] = build_nc()
    nc = _CACHED["nc"]
    res = run_bass_kernel_spmd(nc, in_maps, core_ids=list(range(8)))
    out = np.zeros((B, S, D), dtype=np.float32)
    for c in range(8):
        b, rows = owners[c]
        out[b][rows] = res.results[c]["y"]
    return out


if __name__ == "__main__":
    nc = build_nc()
    print("built OK")
